# version 19
# baseline (speedup 1.0000x reference)
import numpy as np
import ml_dtypes
import concourse.bass as bass
import concourse.mybir as mybir
from concourse.bass_utils import run_bass_kernel_spmd

F32 = mybir.dt.float32
BF16 = mybir.dt.bfloat16
AX = mybir.AxisListType.X
OP = mybir.AluOpType
AF = mybir.ActivationFunctionType

P = 128
D = 1024
NCH = 8
NEG = -1.0e30

ENGINES = ("sync", "gpsimd", "tensor", "vector", "scalar")


class Sched:

    def __init__(self):
        self.q = {e: [] for e in ENGINES}
        self.cnt = {}

    def wait(self, eng, ev):
        if ev is None:
            return
        sem, val = ev
        if val <= 0:
            return
        self.q[eng].append(("w", sem, val))

    def op(self, eng, fn, inc=None, amt=1):
        ev = None
        if inc is not None:
            self.cnt[inc] = self.cnt.get(inc, 0) + amt
            ev = (inc, self.cnt[inc])
        self.q[eng].append(("i", fn, inc, amt))
        return ev

    def play(self, eng, E, sems):
        for rec in self.q[eng]:
            if rec[0] == "w":
                E.wait_ge(sems[rec[1]], rec[2])
            else:
                _, fn, inc, amt = rec
                ins = fn(E)
                if inc is not None:
                    ins.then_inc(sems[inc], amt)


def build_program(ntiles=32, debug=False):
    T = ntiles * P
    NG = ntiles // 4
    nc = bass.Bass("TRN2", target_bir_lowering=False, detect_race_conditions=False)

    x_d = nc.dram_tensor("x", [T, D], F32, kind="ExternalInput").ap()
    wT_d = nc.dram_tensor("wT", [D, 2 * D], F32, kind="ExternalInput").ap()
    wrT_d = nc.dram_tensor("wrT", [D, D], F32, kind="ExternalInput").ap()
    wkT_d = nc.dram_tensor("wkT", [D, D], F32, kind="ExternalInput").ap()
    wvT_d = nc.dram_tensor("wvT", [D, D], F32, kind="ExternalInput").ap()
    mur_d = nc.dram_tensor("mur8", [P, NCH], F32, kind="ExternalInput").ap()
    muk_d = nc.dram_tensor("muk8", [P, NCH], F32, kind="ExternalInput").ap()
    ident_d = nc.dram_tensor("ident", [P, P], BF16, kind="ExternalInput").ap()
    mask_d = nc.dram_tensor("mask", [P, P], F32, kind="ExternalInput").ap()

    y_d = nc.dram_tensor("y", [T, D], F32, kind="ExternalOutput").ap()

    xnfm_scr = nc.dram_tensor("xnfm_scr", [P, ntiles * D], BF16, kind="Internal").ap()
    part_scr = nc.dram_tensor("part_scr", [P, ntiles * D], F32, kind="Internal").ap()

    BASE = 16512
    TOP = 229344
    at = lambda name, shape, dtype, off: nc.alloc_sbuf_tensor_at(name, shape, dtype, offset=off)

    o = BASE
    ident_sb = at("ident_sb", [P, P], BF16, o); o += 256
    mask_sb = at("mask_sb", [P, P], F32, o); o += 512
    ones_sb = at("ones_sb", [1, P], BF16, o); o += 256
    mur_sb = at("mur_sb", [P, NCH], F32, o); o += 32
    muk_sb = at("muk_sb", [P, NCH], F32, o); o += 32
    murb_sb = at("murb_sb", [P, NCH], BF16, o); o += 32
    mukb_sb = at("mukb_sb", [P, NCH], BF16, o); o += 32
    s1_sb = at("s1_sb", [P, ntiles], F32, o); o += ((ntiles * 4 + 31) // 32) * 32
    s2_sb = at("s2_sb", [P, ntiles], F32, o); o += ((ntiles * 4 + 31) // 32) * 32
    mean_sb = at("mean_sb", [P, ntiles], F32, o); o += ((ntiles * 4 + 31) // 32) * 32
    std_sb = at("std_sb", [P, ntiles], F32, o); o += ((ntiles * 4 + 31) // 32) * 32
    rstd_sb = at("rstd_sb", [P, ntiles], F32, o); o += ((ntiles * 4 + 31) // 32) * 32
    a1_sb = at("a1_sb", [P, ntiles], F32, o); o += ((ntiles * 4 + 31) // 32) * 32
    bneg_sb = at("bneg_sb", [P, ntiles], F32, o); o += ((ntiles * 4 + 31) // 32) * 32
    rstdb_sb = at("rstdb_sb", [P, ntiles], BF16, o); o += ((ntiles * 2 + 31) // 32) * 32
    bnegb_sb = at("bnegb_sb", [P, ntiles], BF16, o); o += ((ntiles * 2 + 31) // 32) * 32
    rbrow_sb = at("rbrow_sb", [1, 1024], BF16, o); o += 2048
    NB = 8
    mblk_sb = at("mblk_sb", [P, 1], F32, o); o += 32
    mta_sb = at("mta_sb", [P, 1], F32, o); o += 32
    mtb_sb = at("mtb_sb", [P, 1], F32, o); o += 32
    mrun_sb = at("mrun_sb", [P, 2], F32, o); o += 32
    NSL = 8
    nm_sb = at("nm_sb", [P, NSL], F32, o); o += 32
    dm_sb = at("dm_sb", [P, NSL], F32, o); o += 32
    scale_sb = at("scale_sb", [P, NSL], F32, o); o += 32
    sblk_sb = at("sblk_sb", [P, NSL], F32, o); o += 32
    srun_sb = [at(f"srun_sb{b}", [P, 1], F32, o + 32 * b) for b in range(2)]; o += 64
    rs_sb = at("rs_sb", [P, 1], F32, o); o += 32
    veps_sb = at("veps_sb", [P, ntiles], F32, o); o += ((ntiles * 4 + 31) // 32) * 32
    m2_sb = at("m2_sb", [P, ntiles], F32, o); o += ((ntiles * 4 + 31) // 32) * 32
    mdbg_sb = at("mdbg_sb", [P, ntiles], F32, o); o += ((ntiles * 4 + 31) // 32) * 32
    sdbg_sb = at("sdbg_sb", [P, ntiles], F32, o); o += ((ntiles * 4 + 31) // 32) * 32
    assert o <= BASE + 5632, o

    if True:
        pass

    A0 = BASE + 5632
    kfm_sb = at("kfm_sb", [P, NCH, T], BF16, A0)
    wrt_sb = at("wrt_sb", [P, NCH, D], BF16, A0)
    wkt_sb = at("wkt_sb", [P, NCH, D], BF16, A0 + 16384)
    wvt_sb = at("wvt_sb", [P, NCH, D], BF16, A0 + 32768)
    sqdump_sb = at("sqdump_sb", [P, D], F32, A0 + 49152)
    xntm_sb = [at(f"xntm_sb{b}", [P, D], BF16, A0 + 53248 + 2048 * b) for b in range(2)]
    A_end = A0 + max(NCH * T * 2, 57344)

    B0 = A_end
    vtm_sb = at("vtm_sb", [P, ntiles, D], BF16, B0)
    x_tm = [at(f"x_tm{b}", [P, D], F32, B0 + 4096 * b) for b in range(4)]
    xk_sb = [at(f"xk_sb{b}", [P, NCH, 512], BF16, B0 + 32768 + 8192 * b) for b in range(2)]
    px_tm = x_tm + [at(f"px_tm{b}", [P, D], F32, B0 + 16384 + 4096 * b) for b in range(4)]
    kact_sb = at("kact_sb", [P, NCH, 512], BF16, B0 + 49152)
    part_sb = [at(f"part_sb{b}", [P, D], F32, B0 + 57344 + 4096 * b) for b in range(2)]
    B_end = B0 + max(ntiles * D * 2, 65536)

    C0 = B_end
    wt_sb = at("wt_sb", [P, NCH, 2 * D], BF16, C0)
    C_end = C0 + 32768

    D0 = C_end
    xnq_sb = [at(f"xnq_sb{b}", [P, 4, NCH, P], BF16, D0 + 8192 * b) for b in range(2)]
    out_sb = [at(f"out_sb{b}", [P, D], F32, D0 + 16384 + 4096 * b) for b in range(2)]
    p_sb = [at(f"p_sb{b}", [P, 1024], BF16, D0 + 24576 + 2048 * b) for b in range(2)]
    pt_sb = [at(f"pt_sb{b}", [P, 1024], BF16, D0 + 28672 + 2048 * b) for b in range(2)]
    acc_sb = [at(f"acc_sb{b}", [P, D], F32, D0 + 32768 + 4096 * b) for b in range(2)]
    xnfm_sb = [at(f"xnfm_sb{b}", [P, NCH, P], BF16, D0 + 2048 * b) for b in range(2)]
    xnprev_sb = at("xnprev_sb", [P, NCH, P], BF16, D0 + 4096)
    dmix_sb = at("dmix_sb", [P, NCH, P], BF16, D0 + 6144)
    xr_sb = ([at(f"xr_sb{b}", [P, NCH, P], BF16, D0 + 8192 + 2048 * b) for b in range(2)]
             + [at(f"xr_sb{b+2}", [P, NCH, P], BF16, D0 + 38912 + 2048 * b) for b in range(2)])
    bc_sb = at("bc_sb", [P, 1024], BF16, D0 + 12288)
    tmp_fm = at("tmp_fm", [P, NCH, P], BF16, D0 + 14336)
    rbf_sb = [at(f"rbf_sb{b}", [P, D], BF16, D0 + 16384 + 2048 * b) for b in range(4)]
    relu_sb = at("relu_sb", [P, 512], F32, D0 + 24576)
    relu2_sb = at("relu2_sb", [P, 2, 512], F32, D0 + 34816)
    comb_sb = [at(f"comb_sb{b}", [P, D], F32, D0 + 26624 + 4096 * b) for b in range(2)]
    D_end = D0 + 40960
    assert D_end <= TOP, (D_end, TOP)

    ps_sA = nc.alloc_psum_tensor("ps_sA", [P, 1024], F32)
    ps_sB = nc.alloc_psum_tensor("ps_sB", [P, 1024], F32)
    ps_av = nc.alloc_psum_tensor("ps_av", [P, 1024], F32)
    ps_pt = nc.alloc_psum_tensor("ps_pt", [P, 1024], BF16)
    ps_misc = nc.alloc_psum_tensor("ps_misc", [P, 512], F32)
    assert nc.psum_banks_remaining == 0, nc.psum_banks_remaining
    pair_ps = [ps_sA, ps_sB]

    S = Sched()
    st, sv, ss = "st", "sv", "ss"
    ufree = {"sA": None, "sB": None, "av": None, "pt": None, "misc": None}

    ev_const = None
    for dst, srcd in ((ident_sb, ident_d), (mask_sb, mask_d),
                      (mur_sb, mur_d), (muk_sb, muk_d)):
        ev_const = S.op("sync", lambda E, d=dst, s=srcd: E.dma_start(out=d[:], in_=s), "lc", 16)

    def cast_load(dst, src_ap, sem):
        return S.op("gpsimd", lambda E, d=dst, s=src_ap: E.dma_start(out=d[:], in_=s), sem, 16)

    ev_wr_l = cast_load(wrt_sb, wrT_d.rearrange("(c p) n -> p c n", p=P), "lw")

    ev_x_load = [None] * ntiles
    xbuf_free = [[None, None] for _ in range(4)]
    pxbuf_free = [[None, None] for _ in range(8)]
    ev_red = [None] * ntiles
    ev_sq = [None] * ntiles

    ev_stats_g = [None] * NG

    def stats_tail(gs):
        sl = slice(4 * gs, 4 * gs + 4)
        S.wait("vector", ev_sq[4 * gs + 3])
        ev = S.op("vector", lambda E, s=sl: E.tensor_scalar_mul(mean_sb[:, s], s1_sb[:, s], 1.0 / D))
        ev = S.op("vector", lambda E, s=sl: E.tensor_scalar_mul(std_sb[:, s], s2_sb[:, s], 1.0 / D), sv, 1)
        S.wait("vector", ev)
        ev = S.op("vector", lambda E, s=sl: E.tensor_mul(m2_sb[:, s], mean_sb[:, s], mean_sb[:, s]), sv, 1)
        S.wait("vector", ev)
        ev = S.op("vector", lambda E, s=sl: E.scalar_tensor_tensor(
            veps_sb[:, s], m2_sb[:, s], -1.0, std_sb[:, s], op0=OP.mult, op1=OP.add), sv, 1)
        S.wait("vector", ev)
        ev_veps = S.op("vector", lambda E, s=sl: E.tensor_scalar_add(s1_sb[:, s], veps_sb[:, s], 1e-5), sv, 1)
        S.wait("scalar", ev_veps)
        ev_sqrt = S.op("scalar", lambda E, s=sl: E.activation(std_sb[:, s], s1_sb[:, s], AF.Sqrt), ss, 1)
        S.wait("vector", ev_sqrt)
        ev = S.op("vector", lambda E, s=sl: E.reciprocal(rstd_sb[:, s], std_sb[:, s]), sv, 1)
        S.wait("vector", ev)
        ev = S.op("vector", lambda E, s=sl: E.tensor_scalar_add(a1_sb[:, s], rstd_sb[:, s], 1.0), sv, 1)
        S.wait("vector", ev)
        ev = S.op("vector", lambda E, s=sl: E.scalar_tensor_tensor(
            bneg_sb[:, s], mean_sb[:, s], -1.0, rstd_sb[:, s], op0=OP.mult, op1=OP.mult), sv, 1)
        ev_stats_g[gs] = ev

    S.wait("vector", ev_const)
    S.op("vector", lambda E: E.tensor_copy(murb_sb[:], mur_sb[:]))
    ev_stats = S.op("vector", lambda E: E.tensor_copy(mukb_sb[:], muk_sb[:]), sv, 1)

    def emit_px_load(j):
        b = j % 8
        S.wait("sync", pxbuf_free[b][0])
        S.wait("sync", pxbuf_free[b][1])
        ev_x_load[j] = S.op(
            "sync", lambda E, bb=b, ii=j: E.dma_start(out=px_tm[bb][:], in_=x_d[ii * P:(ii + 1) * P, :]),
            f"lx{b % 4}" if b < 4 else f"px{b - 4}", 16)

    def emit_px_stats(j):
        b = j % 8
        S.wait("vector", ev_x_load[j])
        ev_red[j] = S.op("vector", lambda E, bb=b, ii=j: E.reduce_sum(
            s1_sb[:, ii:ii + 1], px_tm[bb][:], axis=AX), sv, 1)
        S.wait("scalar", ev_x_load[j])
        ev_sq[j] = S.op("scalar", lambda E, bb=b, ii=j: E.activation(
            sqdump_sb[:], px_tm[bb][:], AF.Square,
            accum_out=s2_sb[:, ii:ii + 1]), ss, 1)

    for j in range(8):
        emit_px_load(j)
    for j in range(7):
        emit_px_stats(j)
        if j == 3:
            stats_tail(0)

    ev_xnfm = [None] * ntiles
    ev_xnfm_wr = [None] * ntiles
    ev_part_wr = [None] * ntiles
    ev_xr = [None] * ntiles
    ev_xk_tile = [None] * ntiles
    ev_ffn = [None] * ntiles
    ev_sig = [None] * ntiles
    ev_rmm = [None] * ntiles
    ev_vmm = [None] * ntiles
    ev_kevac = {}
    ev_kmm = {}
    ev_xfm_l = [None] * NG

    S.wait("tensor", ev_stats)
    S.wait("tensor", ev_const)
    S.wait("tensor", ev_wr_l)
    S.wait("vector", ev_const)

    cast_load(wkt_sb, wkT_d.rearrange("(c p) n -> p c n", p=P), "lw2")
    cast_load(wvt_sb, wvT_d.rearrange("(c p) n -> p c n", p=P), "lw3")
    cast_load(wt_sb, wT_d.rearrange("(c p) n -> p c n", p=P), "lw4")

    murb3 = murb_sb[:].unsqueeze(2).broadcast_to([P, NCH, P])
    mukb3 = mukb_sb[:].unsqueeze(2).broadcast_to([P, NCH, P])
    vp_ps = [ps_av, ps_sB, ps_sA, ps_av]

    ev_T = [None] * ntiles
    ev_xnp = [None] * ntiles

    def prep_tile(g, ii):
        i = 4 * g + ii
        bb = i % 2
        S.wait("scalar", ev_x_load[i])
        S.wait("scalar", ev_stats_g[g])
        if i >= 2:
            S.wait("scalar", ev_T[i - 2])
        ev_xntm = S.op("scalar", lambda E, i2=i: E.activation(
            xntm_sb[i2 % 2][:], px_tm[i2 % 8][:], AF.Identity,
            bias=bneg_sb[:, i2:i2 + 1], scale=rstd_sb[:, i2:i2 + 1]), ss, 1)
        S.wait("tensor", ev_xntm)
        S.wait("tensor", ufree["pt"])
        for c in range(NCH):
            ev_T[i] = S.op("tensor", lambda E, i2=i, c2=c: E.transpose(
                ps_pt[:, c2 * P:(c2 + 1) * P], xntm_sb[i2 % 2][:, c2 * P:(c2 + 1) * P],
                ident_sb[:]), st, 1)
        S.wait("scalar", ev_T[i])
        if i >= 2:
            S.wait("scalar", ev_xnfm_wr[i - 2])
        if i >= 1:
            S.wait("scalar", ev_xnp[i - 1])
        dstv = xnfm_sb[bb][:].rearrange("p c t -> p (c t)")
        ev_xnfm[i] = S.op("scalar", lambda E, d2=dstv: E.copy(d2, ps_pt[:]), ss, 1)
        ufree["pt"] = ev_xnfm[i]
        srcv = xnfm_sb[bb][:].rearrange("p c t -> p (c t)")
        ev_xnfm_wr[i] = S.op("scalar", lambda E, ii2=i, s=srcv: E.dma_start(
            out=xnfm_scr[:, ii2 * D:(ii2 + 1) * D], in_=s), f"ws{i % 2}", 16)
        S.wait("vector", ev_xnfm[i])
        if i == 0:
            S.op("vector", lambda E: E.memset(xnprev_sb[:, :, 0:1], 0.0))
        else:
            S.op("vector", lambda E, pb=(i - 1) % 2: E.tensor_copy(
                xnprev_sb[:, :, 0:1], xnfm_sb[pb][:, :, P - 1:P]))
        ev = S.op("vector", lambda E, bb2=bb: E.tensor_copy(
            xnprev_sb[:, :, 1:P], xnfm_sb[bb2][:, :, 0:P - 1]), sv, 1)
        ev_xnp[i] = ev
        S.wait("vector", ev)
        S.op("vector", lambda E, bb2=bb: E.tensor_sub(
            dmix_sb[:], xnfm_sb[bb2][:], xnprev_sb[:]))
        if i >= 4:
            S.wait("vector", ev_rmm[i - 4])
        if g >= 2 and ii == 0:
            S.wait("vector", ev_kmm[(g - 2, NCH - 1)])
        S.op("vector", lambda E, m3=murb3: E.tensor_tensor(
            tmp_fm[:], dmix_sb[:], m3, op=OP.mult))
        ev_xr[i] = S.op("vector", lambda E, b4=i % 4: E.tensor_add(
            xr_sb[b4][:], tmp_fm[:], xnprev_sb[:]), sv, 1)
        S.op("vector", lambda E, m3=mukb3: E.tensor_tensor(
            tmp_fm[:], dmix_sb[:], m3, op=OP.mult))
        ev_xk_tile[i] = S.op("vector", lambda E, gg=g % 2, ii2=ii: E.tensor_add(
            xk_sb[gg][:, :, ii2 * P:(ii2 + 1) * P], tmp_fm[:], xnprev_sb[:]), sv, 1)

    for ii0 in range(4):
        prep_tile(0, ii0)

    for g in range(NG):
        tiles = [4 * g + ii for ii in range(4)]

        for ii in range(4):
            i = tiles[ii]
            bb = i % 2
            S.wait("tensor", ev_xr[i])
            if i >= 1:
                S.wait("tensor", ev_sig[i - 1])
            if g >= 1 and ii == 0:
                S.wait("tensor", ev_ffn[4 * (g - 1) + 2])
            ev = None
            for c in range(NCH):
                for h in range(2):
                    ev = S.op("tensor", lambda E, b4=i % 4, c2=c, h2=h: E.matmul(
                        ps_sA[:, h2 * 512:(h2 + 1) * 512], xr_sb[b4][:, c2, :],
                        wrt_sb[:, c2, h2 * 512:(h2 + 1) * 512],
                        start=(c2 == 0), stop=(c2 == NCH - 1)), st, 1)
            ev_rmm[i] = ev
            S.wait("scalar", ev_rmm[i])
            if i >= 4:
                S.wait("scalar", ev_ffn[i - 4])
            ev_sig[i] = S.op("scalar", lambda E, bb4=i % 4: E.activation(
                rbf_sb[bb4][:], ps_sA[:], AF.Sigmoid), ss, 1)

        if g == 0:
            S.wait("tensor", ("lw2", 16))
            S.wait("tensor", ("lw3", 16))
        S.wait("tensor", ev_xk_tile[tiles[3]])
        if g >= 1:
            S.wait("tensor", ev_ffn[4 * (g - 1) + 1])
        for d in range(NCH):
            if d >= 2:
                S.wait("tensor", ev_kevac[(g, d - 2)])
            elif g >= 1:
                S.wait("tensor", ev_kevac[(g - 1, 6 + d)])
            ev = None
            for c in range(NCH):
                ev = S.op("tensor", lambda E, gg=g % 2, c2=c, d2=d: E.matmul(
                    ps_sB[:, (d2 % 2) * 512:(d2 % 2 + 1) * 512],
                    wkt_sb[:, c2, d2 * P:(d2 + 1) * P],
                    xk_sb[gg][:, c2, :], start=(c2 == 0), stop=(c2 == NCH - 1)), st, 1)
            ev_kmm[(g, d)] = ev
            S.wait("scalar", ev_kmm[(g, d)])
            rb = d % 2
            pk = (g, d - 2) if d >= 2 else ((g - 1, d + 6) if g >= 1 else None)
            if pk is not None and pk in ev_kevac:
                S.wait("scalar", ev_kevac[pk])
            evrl = S.op("scalar", lambda E, d2=d, r2=rb: E.activation(
                relu2_sb[:, r2, :], ps_sB[:, (d2 % 2) * 512:(d2 % 2 + 1) * 512],
                AF.Relu), ss, 1)
            S.wait("scalar", evrl)
            ev_kevac[(g, d)] = S.op("scalar", lambda E, d2=d, r2=rb: E.activation(
                kact_sb[:, d2, :], relu2_sb[:, r2, :], AF.Square), ss, 1)

        for ii in range(4):
            i = tiles[ii]
            bb = i % 2
            ps_v = vp_ps[ii]
            S.wait("tensor", ev_kevac[(g, NCH - 1)])
            if ii == 0 and g >= 1:
                S.wait("tensor", ev_ffn[4 * (g - 1) + 3])
            elif ii == 1:
                S.wait("tensor", ev_kevac[(g, NCH - 1)])
                if True:
                    pass
            elif ii == 2:
                S.wait("tensor", ev_sig[tiles[3]])
            elif ii == 3:
                S.wait("tensor", ev_ffn[tiles[0]])
            ev = None
            for d in range(NCH):
                for h in range(2):
                    ev = S.op("tensor", lambda E, pv=ps_v, ii2=ii, d2=d, h2=h: E.matmul(
                        pv[:, h2 * 512:(h2 + 1) * 512],
                        kact_sb[:, d2, ii2 * P:(ii2 + 1) * P],
                        wvt_sb[:, d2, h2 * 512:(h2 + 1) * 512],
                        start=(d2 == 0), stop=(d2 == NCH - 1)), st, 1)
            ev_vmm[i] = ev

            S.wait("scalar", ev_x_load[i])
            S.wait("scalar", ev_stats_g[g])
            ev_pp = S.op("scalar", lambda E, bb2=bb, ii2=i: E.activation(
                part_sb[bb2][:], px_tm[ii2 % 8][:], AF.Identity,
                bias=bneg_sb[:, ii2:ii2 + 1], scale=a1_sb[:, ii2:ii2 + 1]), ss, 1)
            pxbuf_free[i % 8] = [ev_pp, ev_red[i]]

            if i + 8 < ntiles:
                emit_px_load(i + 8)
            js = i + 7
            if 7 <= js < ntiles:
                emit_px_stats(js)
                if js % 4 == 3:
                    stats_tail(js // 4)

            if g + 1 < NG:
                prep_tile(g + 1, ii)

            S.wait("vector", ev_vmm[i])
            S.wait("vector", ev_sig[i])
            S.wait("vector", ev_pp)
            if i >= 2:
                S.wait("vector", ev_part_wr[i - 2])
            evf = None
            for h in range(2):
                S.op("vector", lambda E, pv=ps_v, bb4=i % 4, h2=h: E.tensor_mul(
                    relu_sb[:], rbf_sb[bb4][:, h2 * 512:(h2 + 1) * 512],
                    pv[:, h2 * 512:(h2 + 1) * 512]))
                evf = S.op("vector", lambda E, bb2=bb, h2=h: E.tensor_add(
                    comb_sb[bb2][:, h2 * 512:(h2 + 1) * 512],
                    part_sb[bb2][:, h2 * 512:(h2 + 1) * 512], relu_sb[:]), sv, 1)
            ev_ffn[i] = evf

            S.wait("scalar", ev_ffn[i])
            ev_part_wr[i] = S.op("scalar", lambda E, ii2=i, bb2=bb: E.dma_start(
                out=part_scr[:, ii2 * D:(ii2 + 1) * D], in_=comb_sb[bb2][:]), f"wp{i % 2}", 16)
    ufree["sA"] = ev_ffn[4 * (NG - 1) + 2]
    ufree["sB"] = ev_ffn[4 * (NG - 1) + 1]
    ufree["av"] = ev_ffn[4 * (NG - 1) + 3]

    if debug:
        d_xk = nc.dram_tensor("d_xk", [P, NCH, 512], BF16, kind="ExternalOutput").ap()
        d_kact = nc.dram_tensor("d_kact", [P, NCH, 512], BF16, kind="ExternalOutput").ap()
        d_rbf = nc.dram_tensor("d_rbf", [P, 2, D], BF16, kind="ExternalOutput").ap()
        S.wait("sync", (sv, S.cnt[sv]))
        S.wait("sync", (ss, S.cnt[ss]))
        S.op("sync", lambda E: E.dma_start(out=d_xk, in_=xk_sb[(NG - 1) % 2][:]), "wd", 16)
        S.op("sync", lambda E: E.dma_start(out=d_kact, in_=kact_sb[:]), "wd", 16)
        S.op("sync", lambda E: E.dma_start(out=d_rbf[:, 0, :], in_=rbf_sb[0][:]), "wd", 16)
        S.op("sync", lambda E: E.dma_start(out=d_rbf[:, 1, :], in_=rbf_sb[1][:]), "wd", 16)

    for sname in ("ws0", "ws1", "wp0", "wp1", "wd"):
        if sname in S.cnt:
            S.wait("sync", (sname, S.cnt[sname]))

    xnq_free = [None, None]
    acc_stage_free = [None, None]
    ev_out_wr = [None] * ntiles
    p_free = [None, None]
    pt_free = [None, None]
    gbc = 0

    def emit_xnq_load(g):
        S.wait("sync", xnq_free[g % 2])
        dst = xnq_sb[g % 2][:].rearrange("p a c t -> p (a c t)")
        return S.op("sync", lambda E, gg=g, d2=dst: E.dma_start(
            out=d2, in_=xnfm_scr[:, gg * 4 * D:(gg + 1) * 4 * D]), f"lq{g % 2}", 16)

    def emit_k_unit(g, kf, ev_xnq_l):
        u = "sA" if kf % 2 == 0 else "sB"
        S.wait("tensor", ev_xnq_l)
        S.wait("tensor", ufree[u])
        ev = None
        for c in range(NCH):
            ev = S.op("tensor", lambda E, gg=g % 2, c2=c, kf2=kf, hp=pair_ps[kf % 2]: E.matmul(
                hp[:, 0:512], wt_sb[:, c2, kf2 * P:(kf2 + 1) * P],
                xnq_sb[gg][:, :, c2, :], start=(c2 == 0), stop=(c2 == NCH - 1)), st, 1)
        S.wait("scalar", ev)
        evk = S.op("scalar", lambda E, gg=g, kf2=kf, hp=pair_ps[kf % 2]: E.copy(
            kfm_sb[:, kf2, gg * 512:(gg + 1) * 512], hp[:, 0:512]), ss, 1)
        ufree[u] = evk

    def emit_v_unit(g, sub, ev_xnq_l):
        S.wait("tensor", ev_xnq_l)
        S.wait("tensor", ufree["av"])
        ev = None
        for c in range(NCH):
            for h in range(2):
                ev = S.op("tensor", lambda E, gg=g % 2, s2=sub, c2=c, h2=h: E.matmul(
                    ps_av[:, h2 * 512:(h2 + 1) * 512], xnq_sb[gg][:, s2, c2, :],
                    wt_sb[:, c2, D + h2 * 512:D + (h2 + 1) * 512],
                    start=(c2 == 0), stop=(c2 == NCH - 1)), st, 1)
        S.wait("scalar", ev)
        evv = S.op("scalar", lambda E, gg=g, s2=sub: E.copy(
            vtm_sb[:, gg * 4 + s2, :], ps_av[:]), ss, 1)
        ufree["av"] = evv

    for g in range(NG):
        tiles = [4 * g + ii for ii in range(4)]

        if g == 0:
            ev_xnq0 = emit_xnq_load(0)
            S.wait("tensor", ("lw4", 16))
            S.wait("tensor", ev_sig[ntiles - 1])
            S.wait("tensor", ev_kevac[(NG - 1, NCH - 1)])
            for kf in range(NCH):
                emit_k_unit(0, kf, ev_xnq0)
            for sub in range(4):
                emit_v_unit(0, sub, ev_xnq0)
        kv_pending = []
        if g + 1 < NG:
            ev_xnqn = emit_xnq_load(g + 1)
            kv_pending = ([("k", kf) for kf in range(NCH)]
                          + [("v", sub) for sub in range(4)])

        def drain_kv(n, g=g, ev_xnqn=(emit_xnq_load if False else None)):
            pass

        def drain_kv(n):
            for _ in range(n):
                if not kv_pending:
                    return
                kind, idx = kv_pending.pop(0)
                if kind == "k":
                    emit_k_unit(g + 1, idx, ev_xnqn)
                else:
                    emit_v_unit(g + 1, idx, ev_xnqn)

        ctxs = []
        flat = []
        for ii in range(4):
            i = tiles[ii]
            nkc_tot = i + 1
            nb = (nkc_tot + 7) // 8
            ct = dict(ii=ii, i=i, nb=nb,
                      blocks=[(8 * j, min(8, nkc_tot - 8 * j)) for j in range(nb)],
                      bpar=[(gbc + j) % 2 for j in range(nb)],
                      slot=[(gbc + j) % 8 for j in range(nb)], tp=i % 2,
                      ev_S=[None] * nb, ev_A=[None] * nb, ev_exp=[None] * nb,
                      ev_max=[None] * nb, ev_fold=[None] * nb, ev_ptv=[None] * nb,
                      ev_part_l=None)
            gbc += nb
            ctxs.append(ct)
            flat += [(ii, j) for j in range(nb)]

        def emit_scores(ct, j, g=g):
            ii, i, nb, tp = ct["ii"], ct["i"], ct["nb"], ct["tp"]
            kc0, nkc = ct["blocks"][j]
            w = nkc * P
            jp = ct["bpar"][j]
            diag = (j == nb - 1)
            un = "sA" if jp == 0 else "sB"
            S.wait("tensor", ufree[un])
            wmm = w - P if diag else w
            spans = []
            if wmm > 0:
                spans.append((0, min(512, wmm)))
                if wmm > 512:
                    spans.append((512, wmm - 512))
            ev = None
            ev_span = [None] * max(1, len(spans))
            for c in range(NCH):
                for si, (off, n) in enumerate(spans):
                    ev = S.op("tensor", lambda E, gg=g % 2, ii3=ii, c2=c, o2=off,
                              n2=n, jp2=jp, kc=kc0: E.matmul(
                        pair_ps[jp2][:, o2:o2 + n2], xnq_sb[gg][:, ii3, c2, :],
                        kfm_sb[:, c2, kc * P + o2:kc * P + o2 + n2],
                        start=(c2 == 0), stop=(c2 == NCH - 1)), st, 1)
                    if c == NCH - 1:
                        ev_span[si] = ev
            if diag:
                S.wait("tensor", ufree["misc"])
                for c in range(NCH):
                    ev = S.op("tensor", lambda E, gg=g % 2, ii3=ii, c2=c, kd=i: E.matmul(
                        ps_misc[:, 0:P], xnq_sb[gg][:, ii3, c2, :],
                        kfm_sb[:, c2, kd * P:(kd + 1) * P],
                        start=(c2 == 0), stop=(c2 == NCH - 1)), st, 1)
            ct["ev_S"][j] = ev
            S.wait("vector", ct["ev_S"][j])
            if diag:
                doff = (i - kc0) * P
                evm = S.op("vector", lambda E, jp2=jp, d2=doff: E.tensor_add(
                    pair_ps[jp2][:, d2:d2 + P], ps_misc[:, 0:P], mask_sb[:]), sv, 1)
                ufree["misc"] = evm
                S.wait("vector", evm)
            mt = mrun_sb[:, j % 2:j % 2 + 1]
            if j == 0:
                ev = S.op("vector", lambda E, jp2=jp, w2=w, m2=mt: E.reduce_max(
                    m2, pair_ps[jp2][:, 0:w2], axis=AX), sv, 1)
                S.wait("vector", ev)
                ct["ev_max"][j] = S.op("vector", lambda E, sl=ct["slot"][j], m2=mt: E.tensor_scalar_mul(
                    nm_sb[:, sl:sl + 1], m2, -1.0), sv, 1)
            else:
                mo = mrun_sb[:, (j - 1) % 2:(j - 1) % 2 + 1]
                ev = S.op("vector", lambda E, jp2=jp, w2=w: E.reduce_max(
                    mblk_sb[:], pair_ps[jp2][:, 0:w2], axis=AX), sv, 1)
                S.wait("vector", ev)
                ev = S.op("vector", lambda E, m2=mt, mo2=mo: E.tensor_max(
                    m2, mo2, mblk_sb[:]), sv, 1)
                S.wait("vector", ev)
                ev = S.op("vector", lambda E, sl=ct["slot"][j], m2=mt, mo2=mo: E.tensor_sub(
                    dm_sb[:, sl:sl + 1], mo2, m2), sv, 1)
                S.wait("vector", ev)
                ct["ev_max"][j] = S.op("vector", lambda E, sl=ct["slot"][j], m2=mt: E.tensor_scalar_mul(
                    nm_sb[:, sl:sl + 1], m2, -1.0), sv, 1)
            S.wait("scalar", ct["ev_max"][j])
            if j > 0:
                S.op("scalar", lambda E, sl=ct["slot"][j]: E.activation(
                    scale_sb[:, sl:sl + 1], dm_sb[:, sl:sl + 1], AF.Exp))
            S.wait("scalar", p_free[jp])
            ct["ev_exp"][j] = S.op("scalar", lambda E, jp2=jp, w2=w, sl=ct["slot"][j]: E.activation(
                p_sb[jp2][:, 0:w2], pair_ps[jp2][:, 0:w2], AF.Exp,
                bias=nm_sb[:, sl:sl + 1], scale=1.0,
                accum_out=sblk_sb[:, sl:sl + 1]), ss, 1)
            ufree[un] = ct["ev_exp"][j]

        def emit_t(ct, j):
            if j == 0:
                i = ct["i"]
                if i >= 2:
                    S.wait("sync", ev_out_wr[i - 2])
                ct["ev_part_l"] = S.op("sync", lambda E, ii2=i: E.dma_start(
                    out=out_sb[ii2 % 2][:], in_=part_scr[:, ii2 * D:(ii2 + 1) * D]),
                    f"lp{i % 2}", 16)
            kc0, nkc = ct["blocks"][j]
            jp = ct["bpar"][j]
            h0n = (nkc + 1) // 2
            S.wait("tensor", ct["ev_exp"][j])
            S.wait("tensor", ufree["pt"])
            ev = None
            for kc in range(nkc):
                ev = S.op("tensor", lambda E, jp2=jp, k2=kc: E.transpose(
                    ps_pt[:, k2 * P:(k2 + 1) * P], p_sb[jp2][:, k2 * P:(k2 + 1) * P],
                    ident_sb[:]), st, 1)
            p_free[jp] = ev
            S.wait("vector", ev)
            S.wait("vector", pt_free[jp])
            ev_h0 = S.op("vector", lambda E, jp2=jp, w2=h0n * P: E.tensor_copy(
                pt_sb[jp2][:, 0:w2], ps_pt[:, 0:w2]), sv, 1)
            if nkc > h0n:
                ev_h1 = S.op("vector", lambda E, jp2=jp, a=h0n * P, w2=nkc * P: E.tensor_copy(
                    pt_sb[jp2][:, a:w2], ps_pt[:, a:w2]), sv, 1)
            else:
                ev_h1 = ev_h0
            ct["ev_ptv"][j] = (ev_h0, ev_h1, h0n)
            ufree["pt"] = ev_h1

        def emit_a(ct, j):
            ii, i, nb, tp = ct["ii"], ct["i"], ct["nb"], ct["tp"]
            kc0, nkc = ct["blocks"][j]
            jp = ct["bpar"][j]
            ev_h0, ev_h1, h0n = ct["ev_ptv"][j]
            S.wait("tensor", ev_h0)
            S.wait("tensor", ufree["av"])
            ev = None
            for kc in range(nkc):
                if kc == h0n:
                    S.wait("tensor", ev_h1)
                for h in range(2):
                    ev = S.op("tensor", lambda E, jp2=jp, k2=kc, h2=h, kg=kc0 + kc: E.matmul(
                        ps_av[:, h2 * 512:(h2 + 1) * 512], pt_sb[jp2][:, k2 * P:(k2 + 1) * P],
                        vtm_sb[:, kg, h2 * 512:(h2 + 1) * 512],
                        start=(k2 == 0), stop=(k2 == nkc - 1)), st, 1)
            ct["ev_A"][j] = ev
            pt_free[jp] = ev
            if nb > 1:
                S.wait("vector", ct["ev_A"][j])
                if j == 0:
                    S.wait("vector", acc_stage_free[0])
                    acc_stage_free[0] = None
                    ev = S.op("vector", lambda E: E.tensor_copy(acc_sb[0][:], ps_av[:]), sv, 1)
                    S.wait("vector", ev)
                    ct["ev_fold"][j] = S.op("vector", lambda E, sl=ct["slot"][0]: E.tensor_copy(
                        srun_sb[0][:], sblk_sb[:, sl:sl + 1]), sv, 1)
                    S.wait("vector", ct["ev_fold"][j])
                else:
                    S.wait("vector", acc_stage_free[j % 2])
                    acc_stage_free[j % 2] = None
                    ev = S.op("vector", lambda E, j2=j, sl=ct["slot"][j]: E.scalar_tensor_tensor(
                        acc_sb[j2 % 2][:], acc_sb[(j2 - 1) % 2][:],
                        scale_sb[:, sl:sl + 1], ps_av[:],
                        op0=OP.mult, op1=OP.add), sv, 1)
                    S.wait("vector", ev)
                    ct["ev_fold"][j] = S.op("vector", lambda E, j2=j, sl=ct["slot"][j]: E.scalar_tensor_tensor(
                        srun_sb[j2 % 2][:], srun_sb[(j2 - 1) % 2][:],
                        scale_sb[:, sl:sl + 1],
                        sblk_sb[:, sl:sl + 1], op0=OP.mult, op1=OP.add), sv, 1)
                    S.wait("vector", ct["ev_fold"][j])
                ufree["av"] = ct["ev_fold"][j]

        def emit_final(ct):
            ii, i, nb, tp = ct["ii"], ct["i"], ct["nb"], ct["tp"]
            S.wait("vector", ct["ev_part_l"])
            stage = (i % 2) if nb == 1 else (nb % 2)
            S.wait("vector", acc_stage_free[stage])
            acc_stage_free[stage] = None
            if nb == 1:
                S.wait("vector", ct["ev_A"][0])
                ev = S.op("vector", lambda E, sl=ct["slot"][0]: E.reciprocal(rs_sb[:], sblk_sb[:, sl:sl + 1]), sv, 1)
                S.wait("vector", ev)
                ev_out = S.op("vector", lambda E, sg=stage, bb2=i % 2: E.scalar_tensor_tensor(
                    acc_sb[sg][:], ps_av[:], rs_sb[:], out_sb[bb2][:],
                    op0=OP.mult, op1=OP.add), sv, 1)
                ufree["av"] = ev_out
            else:
                ev = S.op("vector", lambda E, pl=(nb - 1) % 2: E.reciprocal(rs_sb[:], srun_sb[pl][:]), sv, 1)
                S.wait("vector", ev)
                ev_out = S.op("vector", lambda E, sg=stage, bb2=i % 2, pl=(nb - 1) % 2: E.scalar_tensor_tensor(
                    acc_sb[sg][:], acc_sb[pl][:], rs_sb[:], out_sb[bb2][:],
                    op0=OP.mult, op1=OP.add), sv, 1)
            S.wait("scalar", ev_out)
            ev_out_wr[i] = S.op("scalar", lambda E, ii2=i, sg=stage: E.dma_start(
                out=y_d[ii2 * P:(ii2 + 1) * P, :], in_=acc_sb[sg][:]), f"wo{i % 2}", 16)
            acc_stage_free[stage] = ev_out_wr[i]

        emit_scores(ctxs[flat[0][0]], flat[0][1])
        if len(flat) > 1:
            emit_scores(ctxs[flat[1][0]], flat[1][1])
        per_step = max(1, -(-len(kv_pending) // max(1, len(flat) - 2)))
        for b, (ii, j) in enumerate(flat):
            ct = ctxs[ii]
            emit_t(ct, j)
            if b + 2 < len(flat):
                ii2, j2 = flat[b + 2]
                emit_scores(ctxs[ii2], j2)
            emit_a(ct, j)
            if b >= 2:
                drain_kv(per_step)
            if j == ct["nb"] - 1:
                emit_final(ct)
        drain_kv(len(kv_pending))

        last_ct = ctxs[3]
        xnq_free[g % 2] = last_ct["ev_S"][last_ct["nb"] - 1]

    if debug:
        for sname in ("wo0", "wo1"):
            if sname in S.cnt:
                S.wait("sync", (sname, S.cnt[sname]))
        S.wait("sync", (sv, S.cnt[sv]))
        d_mean = nc.dram_tensor("d_mean", [P, ntiles], F32, kind="ExternalOutput").ap()
        d_rstd = nc.dram_tensor("d_rstd", [P, ntiles], F32, kind="ExternalOutput").ap()
        d_s2 = nc.dram_tensor("d_s2", [P, ntiles], F32, kind="ExternalOutput").ap()
        d_veps = nc.dram_tensor("d_veps", [P, ntiles], F32, kind="ExternalOutput").ap()
        S.op("sync", lambda E: E.dma_start(out=d_mean, in_=mean_sb[:]), "wd", 16)
        S.op("sync", lambda E: E.dma_start(out=d_rstd, in_=rstd_sb[:]), "wd", 16)
        S.op("sync", lambda E: E.dma_start(out=d_s2, in_=s2_sb[:]), "wd", 16)
        S.op("sync", lambda E: E.dma_start(out=d_veps, in_=veps_sb[:]), "wd", 16)
        d_m2 = nc.dram_tensor("d_m2", [P, ntiles], F32, kind="ExternalOutput").ap()
        d_std = nc.dram_tensor("d_std", [P, ntiles], F32, kind="ExternalOutput").ap()
        S.op("sync", lambda E: E.dma_start(out=d_m2, in_=m2_sb[:]), "wd", 16)
        S.op("sync", lambda E: E.dma_start(out=d_std, in_=std_sb[:]), "wd", 16)
        d_xnfm = nc.dram_tensor("d_xnfm", [P, ntiles * D], BF16, kind="ExternalOutput").ap()
        d_part = nc.dram_tensor("d_part", [P, ntiles * D], F32, kind="ExternalOutput").ap()
        d_kfm = nc.dram_tensor("d_kfm", [P, NCH, T], BF16, kind="ExternalOutput").ap()
        d_vtm = nc.dram_tensor("d_vtm", [P, ntiles, D], BF16, kind="ExternalOutput").ap()
        for sname in ("wo0", "wo1"):
            if sname in S.cnt:
                S.wait("sync", (sname, S.cnt[sname]))
        S.wait("sync", (sv, S.cnt[sv]))
        S.op("sync", lambda E: E.dma_start(out=d_xnfm, in_=xnfm_scr[:, :]), "wd", 16)
        S.op("sync", lambda E: E.dma_start(out=d_part, in_=part_scr[:, :]), "wd", 16)
        S.op("sync", lambda E: E.dma_start(out=d_kfm, in_=kfm_sb[:]), "wd", 16)
        S.op("sync", lambda E: E.dma_start(out=d_vtm, in_=vtm_sb[:]), "wd", 16)

    for sname in ("wo0", "wo1", "wd"):
        if sname in S.cnt:
            S.wait("sync", (sname, S.cnt[sname]))

    from contextlib import ExitStack
    with ExitStack() as ctx:
        e = ctx.enter_context
        names = ["lc", "lw", "lw2", "lw3", "lw4", "st", "sv", "ss", "wd",
                 "px0", "px1", "px2", "px3",
                 "lx0", "lx1", "lx2", "lx3", "lf0", "lf1", "lq0", "lq1",
                 "lp0", "lp1", "ws0", "ws1", "wp0", "wp1", "wo0", "wo1"]
        sems = {n: e(nc.semaphore(n)) for n in names if n in S.cnt or n in
                {w[1] for q in S.q.values() for w in q if w[0] == "w"}}
        block = e(nc.Block())
        block.sync(lambda E: S.play("sync", E, sems))
        block.gpsimd(lambda E: S.play("gpsimd", E, sems))
        block.tensor(lambda E: S.play("tensor", E, sems))
        block.vector(lambda E: S.play("vector", E, sems))
        block.scalar(lambda E: S.play("scalar", E, sems))
    return nc


_NC_CACHE = {}


def kernel(L0_attn_x, L0_attn_W, L0_ffn_W_r, L0_ffn_W_k, L0_ffn_W_v,
           L0_ffn_mu_r, L0_ffn_mu_k):
    x = np.ascontiguousarray(np.asarray(L0_attn_x, dtype=np.float32))
    B, T, Dm = x.shape
    ntiles = T // P
    w = np.asarray(L0_attn_W, dtype=np.float32)
    wrT = np.ascontiguousarray(np.asarray(L0_ffn_W_r, dtype=np.float32).T)
    wkT = np.ascontiguousarray(np.asarray(L0_ffn_W_k, dtype=np.float32).T)
    wvT = np.ascontiguousarray(np.asarray(L0_ffn_W_v, dtype=np.float32).T)
    wT = np.ascontiguousarray(w.T)
    mur8 = np.ascontiguousarray(np.asarray(L0_ffn_mu_r, dtype=np.float32).reshape(NCH, P).T)
    muk8 = np.ascontiguousarray(np.asarray(L0_ffn_mu_k, dtype=np.float32).reshape(NCH, P).T)
    ident = np.eye(P, dtype=ml_dtypes.bfloat16)
    mask = np.triu(np.full((P, P), NEG, dtype=np.float32), k=1)

    if ntiles not in _NC_CACHE:
        _NC_CACHE[ntiles] = build_program(ntiles)
    nc = _NC_CACHE[ntiles]

    in_maps = []
    for b in range(B):
        in_maps.append(dict(
            x=x[b], wT=wT, wrT=wrT, wkT=wkT,
            wvT=wvT, mur8=mur8, muk8=muk8, ident=ident, mask=mask))
    res = run_bass_kernel_spmd(nc, in_maps, core_ids=list(range(B)))
    return np.stack([np.asarray(r["y"]) for r in res.results], axis=0)


if __name__ == "__main__":
    rng = np.random.default_rng(0)
    B, T = 2, 512
    xs = rng.standard_normal((B, T, D), dtype=np.float32)
    W = rng.standard_normal((2 * D, D), dtype=np.float32) * 0.02
    s = 1.0 / np.sqrt(D)
    Wr = rng.uniform(-s, s, (D, D)).astype(np.float32)
    Wk = rng.uniform(-s, s, (D, D)).astype(np.float32)
    Wv = rng.uniform(-s, s, (D, D)).astype(np.float32)
    mur = rng.uniform(0, 1, D).astype(np.float32)
    muk = rng.uniform(0, 1, D).astype(np.float32)
    out = kernel(xs, W, Wr, Wk, Wv, mur, muk)
    print(out.shape, out.dtype)


# version 20
# speedup vs baseline: 1.1295x; 1.1295x over previous
import numpy as np
import ml_dtypes
import concourse.bass as bass
import concourse.mybir as mybir
from concourse.bass_utils import run_bass_kernel_spmd

F32 = mybir.dt.float32
BF16 = mybir.dt.bfloat16
AX = mybir.AxisListType.X
OP = mybir.AluOpType
AF = mybir.ActivationFunctionType

P = 128
D = 1024
NCH = 8
NEG = -1.0e30

ENGINES = ("sync", "gpsimd", "tensor", "vector", "scalar")


class Sched:

    def __init__(self):
        self.q = {e: [] for e in ENGINES}
        self.cnt = {}

    def wait(self, eng, ev):
        if ev is None:
            return
        sem, val = ev
        if val <= 0:
            return
        self.q[eng].append(("w", sem, val))

    def op(self, eng, fn, inc=None, amt=1):
        ev = None
        if inc is not None:
            self.cnt[inc] = self.cnt.get(inc, 0) + amt
            ev = (inc, self.cnt[inc])
        self.q[eng].append(("i", fn, inc, amt))
        return ev

    def play(self, eng, E, sems):
        for rec in self.q[eng]:
            if rec[0] == "w":
                E.wait_ge(sems[rec[1]], rec[2])
            else:
                _, fn, inc, amt = rec
                ins = fn(E)
                if inc is not None:
                    ins.then_inc(sems[inc], amt)


def build_program(ntiles=32, debug=False):
    T = ntiles * P
    NG = ntiles // 4
    nc = bass.Bass("TRN2", target_bir_lowering=False, detect_race_conditions=False)

    x_d = nc.dram_tensor("x", [T, D], F32, kind="ExternalInput").ap()
    wT_d = nc.dram_tensor("wT", [D, 2 * D], F32, kind="ExternalInput").ap()
    wrT_d = nc.dram_tensor("wrT", [D, D], F32, kind="ExternalInput").ap()
    wkT_d = nc.dram_tensor("wkT", [D, D], F32, kind="ExternalInput").ap()
    wvT_d = nc.dram_tensor("wvT", [D, D], F32, kind="ExternalInput").ap()
    mur_d = nc.dram_tensor("mur8", [P, NCH], F32, kind="ExternalInput").ap()
    muk_d = nc.dram_tensor("muk8", [P, NCH], F32, kind="ExternalInput").ap()
    ident_d = nc.dram_tensor("ident", [P, P], BF16, kind="ExternalInput").ap()
    mask_d = nc.dram_tensor("mask", [P, P], F32, kind="ExternalInput").ap()

    y_d = nc.dram_tensor("y", [T, D], F32, kind="ExternalOutput").ap()

    xnfm_scr = nc.dram_tensor("xnfm_scr", [P, ntiles * D], BF16, kind="Internal").ap()
    part_scr = nc.dram_tensor("part_scr", [P, ntiles * D], F32, kind="Internal").ap()

    BASE = 16512
    TOP = 229344
    at = lambda name, shape, dtype, off: nc.alloc_sbuf_tensor_at(name, shape, dtype, offset=off)

    o = BASE
    ident_sb = at("ident_sb", [P, P], BF16, o); o += 256
    mask_sb = at("mask_sb", [P, P], F32, o); o += 512
    ones_sb = at("ones_sb", [1, P], BF16, o); o += 256
    mur_sb = at("mur_sb", [P, NCH], F32, o); o += 32
    muk_sb = at("muk_sb", [P, NCH], F32, o); o += 32
    murb_sb = at("murb_sb", [P, NCH], BF16, o); o += 32
    mukb_sb = at("mukb_sb", [P, NCH], BF16, o); o += 32
    s1_sb = at("s1_sb", [P, ntiles], F32, o); o += ((ntiles * 4 + 31) // 32) * 32
    s2_sb = at("s2_sb", [P, ntiles], F32, o); o += ((ntiles * 4 + 31) // 32) * 32
    mean_sb = at("mean_sb", [P, ntiles], F32, o); o += ((ntiles * 4 + 31) // 32) * 32
    std_sb = at("std_sb", [P, ntiles], F32, o); o += ((ntiles * 4 + 31) // 32) * 32
    rstd_sb = at("rstd_sb", [P, ntiles], F32, o); o += ((ntiles * 4 + 31) // 32) * 32
    a1_sb = at("a1_sb", [P, ntiles], F32, o); o += ((ntiles * 4 + 31) // 32) * 32
    bneg_sb = at("bneg_sb", [P, ntiles], F32, o); o += ((ntiles * 4 + 31) // 32) * 32
    rstdb_sb = at("rstdb_sb", [P, ntiles], BF16, o); o += ((ntiles * 2 + 31) // 32) * 32
    bnegb_sb = at("bnegb_sb", [P, ntiles], BF16, o); o += ((ntiles * 2 + 31) // 32) * 32
    rbrow_sb = at("rbrow_sb", [1, 1024], BF16, o); o += 2048
    NB = 8
    mblk_sb = at("mblk_sb", [P, 1], F32, o); o += 32
    mta_sb = at("mta_sb", [P, 1], F32, o); o += 32
    mtb_sb = at("mtb_sb", [P, 1], F32, o); o += 32
    mrun_sb = at("mrun_sb", [P, 2], F32, o); o += 32
    NSL = 8
    nm_sb = at("nm_sb", [P, NSL], F32, o); o += 32
    dm_sb = at("dm_sb", [P, NSL], F32, o); o += 32
    scale_sb = at("scale_sb", [P, NSL], F32, o); o += 32
    sblk_sb = at("sblk_sb", [P, NSL], F32, o); o += 32
    srun_sb = [at(f"srun_sb{b}", [P, 1], F32, o + 32 * b) for b in range(2)]; o += 64
    rs_sb = at("rs_sb", [P, 1], F32, o); o += 32
    veps_sb = at("veps_sb", [P, ntiles], F32, o); o += ((ntiles * 4 + 31) // 32) * 32
    m2_sb = at("m2_sb", [P, ntiles], F32, o); o += ((ntiles * 4 + 31) // 32) * 32
    mdbg_sb = at("mdbg_sb", [P, ntiles], F32, o); o += ((ntiles * 4 + 31) // 32) * 32
    sdbg_sb = at("sdbg_sb", [P, ntiles], F32, o); o += ((ntiles * 4 + 31) // 32) * 32
    assert o <= BASE + 5632, o

    if True:
        pass

    A0 = BASE + 5632
    kfm_sb = at("kfm_sb", [P, NCH, T], BF16, A0)
    wrt_sb = at("wrt_sb", [P, NCH, D], BF16, A0)
    wkt_sb = at("wkt_sb", [P, NCH, D], BF16, A0 + 16384)
    wvt_sb = at("wvt_sb", [P, NCH, D], BF16, A0 + 32768)
    sqdump_sb = at("sqdump_sb", [P, D], F32, A0 + 49152)
    xntm_sb = [at(f"xntm_sb{b}", [P, D], BF16, A0 + 53248 + 2048 * b) for b in range(2)]
    xnfm_hi = [at(f"xnfm_hi{b}", [P, NCH, P], BF16, A0 + 57344 + 2048 * b) for b in range(2)]
    A_end = A0 + max(NCH * T * 2, 61440)

    B0 = A_end
    vtm_sb = at("vtm_sb", [P, ntiles, D], BF16, B0)
    x_tm = [at(f"x_tm{b}", [P, D], F32, B0 + 4096 * b) for b in range(4)]
    xk_sb = [at(f"xk_sb{b}", [P, NCH, 512], BF16, B0 + 32768 + 8192 * b) for b in range(2)]
    px_tm = x_tm + [at(f"px_tm{b}", [P, D], F32, B0 + 16384 + 4096 * b) for b in range(4)]
    kact_sb = at("kact_sb", [P, NCH, 512], BF16, B0 + 49152)
    part_sb = [at(f"part_sb{b}", [P, D], F32, B0 + 57344 + 4096 * b) for b in range(2)]
    B_end = B0 + max(ntiles * D * 2, 65536)

    C0 = B_end
    wt_sb = at("wt_sb", [P, NCH, 2 * D], BF16, C0)
    C_end = C0 + 32768

    D0 = C_end
    xnq_sb = [at(f"xnq_sb{b}", [P, 4, NCH, P], BF16, D0 + 8192 * b) for b in range(2)]
    out_sb = [at(f"out_sb{b}", [P, D], F32, D0 + 16384 + 4096 * b) for b in range(2)]
    p_sb = [at(f"p_sb{b}", [P, 1024], BF16, D0 + 24576 + 2048 * b) for b in range(2)]
    pt_sb = [at(f"pt_sb{b}", [P, 1024], BF16, D0 + 28672 + 2048 * b) for b in range(2)]
    acc_sb = [at(f"acc_sb{b}", [P, D], F32, D0 + 32768 + 4096 * b) for b in range(2)]
    xnfm_sb = [at(f"xnfm_sb{b}", [P, NCH, P], BF16, D0 + 2048 * b) for b in range(2)]
    xnprev_sb = at("xnprev_sb", [P, NCH, P], BF16, D0 + 4096)
    dmix_sb = at("dmix_sb", [P, NCH, P], BF16, D0 + 6144)
    xr_sb = ([at(f"xr_sb{b}", [P, NCH, P], BF16, D0 + 8192 + 2048 * b) for b in range(2)]
             + [at(f"xr_sb{b+2}", [P, NCH, P], BF16, D0 + 38912 + 2048 * b) for b in range(2)])
    bc_sb = at("bc_sb", [P, 1024], BF16, D0 + 12288)
    tmp_fm = at("tmp_fm", [P, NCH, P], BF16, D0 + 14336)
    rbf_sb = [at(f"rbf_sb{b}", [P, D], BF16, D0 + 16384 + 2048 * b) for b in range(4)]
    relu_sb = at("relu_sb", [P, 512], F32, D0 + 24576)
    relu2_sb = at("relu2_sb", [P, 2, 512], F32, D0 + 34816)
    comb_sb = [at(f"comb_sb{b}", [P, D], F32, D0 + 26624 + 4096 * b) for b in range(2)]
    D_end = D0 + 40960
    assert D_end <= TOP, (D_end, TOP)
    xnfm_sb = xnfm_sb + xnfm_hi

    ps_sA = nc.alloc_psum_tensor("ps_sA", [P, 1024], F32)
    ps_sB = nc.alloc_psum_tensor("ps_sB", [P, 1024], F32)
    ps_av = nc.alloc_psum_tensor("ps_av", [P, 1024], F32)
    ps_pt = nc.alloc_psum_tensor("ps_pt", [P, 1024], BF16)
    ps_misc = nc.alloc_psum_tensor("ps_misc", [P, 512], F32)
    assert nc.psum_banks_remaining == 0, nc.psum_banks_remaining
    pair_ps = [ps_sA, ps_sB]

    S = Sched()
    st, sv, ss = "st", "sv", "ss"
    ufree = {"sA": None, "sB": None, "av": None, "pt": None, "misc": None}

    ev_const = None
    for dst, srcd in ((ident_sb, ident_d), (mask_sb, mask_d),
                      (mur_sb, mur_d), (muk_sb, muk_d)):
        ev_const = S.op("sync", lambda E, d=dst, s=srcd: E.dma_start(out=d[:], in_=s), "lc", 16)

    def cast_load(dst, src_ap, sem):
        return S.op("gpsimd", lambda E, d=dst, s=src_ap: E.dma_start(out=d[:], in_=s), sem, 16)

    ev_wr_l = cast_load(wrt_sb, wrT_d.rearrange("(c p) n -> p c n", p=P), "lw")

    ev_x_load = [None] * ntiles
    xbuf_free = [[None, None] for _ in range(4)]
    pxbuf_free = [[None, None] for _ in range(8)]
    ev_red = [None] * ntiles
    ev_sq = [None] * ntiles

    ev_stats_g = [None] * NG

    def stats_tail(gs):
        sl = slice(4 * gs, 4 * gs + 4)
        S.wait("vector", ev_sq[4 * gs + 3])
        ev = S.op("vector", lambda E, s=sl: E.tensor_scalar_mul(mean_sb[:, s], s1_sb[:, s], 1.0 / D))
        ev = S.op("vector", lambda E, s=sl: E.tensor_scalar_mul(std_sb[:, s], s2_sb[:, s], 1.0 / D), sv, 1)
        S.wait("vector", ev)
        ev = S.op("vector", lambda E, s=sl: E.tensor_mul(m2_sb[:, s], mean_sb[:, s], mean_sb[:, s]), sv, 1)
        S.wait("vector", ev)
        ev = S.op("vector", lambda E, s=sl: E.scalar_tensor_tensor(
            veps_sb[:, s], m2_sb[:, s], -1.0, std_sb[:, s], op0=OP.mult, op1=OP.add), sv, 1)
        S.wait("vector", ev)
        ev_veps = S.op("vector", lambda E, s=sl: E.tensor_scalar_add(s1_sb[:, s], veps_sb[:, s], 1e-5), sv, 1)
        S.wait("scalar", ev_veps)
        ev_sqrt = S.op("scalar", lambda E, s=sl: E.activation(std_sb[:, s], s1_sb[:, s], AF.Sqrt), ss, 1)
        S.wait("vector", ev_sqrt)
        ev = S.op("vector", lambda E, s=sl: E.reciprocal(rstd_sb[:, s], std_sb[:, s]), sv, 1)
        S.wait("vector", ev)
        ev = S.op("vector", lambda E, s=sl: E.tensor_scalar_add(a1_sb[:, s], rstd_sb[:, s], 1.0), sv, 1)
        S.wait("vector", ev)
        ev = S.op("vector", lambda E, s=sl: E.scalar_tensor_tensor(
            bneg_sb[:, s], mean_sb[:, s], -1.0, rstd_sb[:, s], op0=OP.mult, op1=OP.mult), sv, 1)
        ev_stats_g[gs] = ev

    S.wait("vector", ev_const)
    S.op("vector", lambda E: E.tensor_copy(murb_sb[:], mur_sb[:]))
    ev_stats = S.op("vector", lambda E: E.tensor_copy(mukb_sb[:], muk_sb[:]), sv, 1)

    def emit_px_load(j):
        b = j % 8
        S.wait("sync", pxbuf_free[b][0])
        S.wait("sync", pxbuf_free[b][1])
        ev_x_load[j] = S.op(
            "sync", lambda E, bb=b, ii=j: E.dma_start(out=px_tm[bb][:], in_=x_d[ii * P:(ii + 1) * P, :]),
            f"lx{b % 4}" if b < 4 else f"px{b - 4}", 16)

    def emit_px_stats(j):
        b = j % 8
        S.wait("vector", ev_x_load[j])
        ev_red[j] = S.op("vector", lambda E, bb=b, ii=j: E.reduce_sum(
            s1_sb[:, ii:ii + 1], px_tm[bb][:], axis=AX), sv, 1)
        S.wait("scalar", ev_x_load[j])
        ev_sq[j] = S.op("scalar", lambda E, bb=b, ii=j: E.activation(
            sqdump_sb[:], px_tm[bb][:], AF.Square,
            accum_out=s2_sb[:, ii:ii + 1]), ss, 1)

    for j in range(8):
        emit_px_load(j)
    for j in range(7):
        emit_px_stats(j)
        if j == 3:
            stats_tail(0)

    ev_xnfm = [None] * ntiles
    ev_xnfm_wr = [None] * ntiles
    ev_part_wr = [None] * ntiles
    ev_xr = [None] * ntiles
    ev_xk_tile = [None] * ntiles
    ev_ffn = [None] * ntiles
    ev_sig = [None] * ntiles
    ev_rmm = [None] * ntiles
    ev_vmm = [None] * ntiles
    ev_kevac = {}
    ev_kmm = {}
    ev_xfm_l = [None] * NG

    S.wait("tensor", ev_stats)
    S.wait("tensor", ev_const)
    S.wait("tensor", ev_wr_l)
    S.wait("vector", ev_const)

    cast_load(wkt_sb, wkT_d.rearrange("(c p) n -> p c n", p=P), "lw2")
    cast_load(wvt_sb, wvT_d.rearrange("(c p) n -> p c n", p=P), "lw3")
    cast_load(wt_sb, wT_d.rearrange("(c p) n -> p c n", p=P), "lw4")

    murb3 = murb_sb[:].unsqueeze(2).broadcast_to([P, NCH, P])
    mukb3 = mukb_sb[:].unsqueeze(2).broadcast_to([P, NCH, P])
    vp_ps = [ps_av, ps_sB, ps_sA, ps_av]

    ev_T = [None] * ntiles
    ev_xnp = [None] * ntiles

    def prep_scalar(g, ii):
        i = 4 * g + ii
        b4 = i % 4
        S.wait("scalar", ev_x_load[i])
        S.wait("scalar", ev_stats_g[g])
        if i >= 2:
            S.wait("scalar", ev_T[i - 2])
        ev_xntm = S.op("scalar", lambda E, i2=i: E.activation(
            xntm_sb[i2 % 2][:], px_tm[i2 % 8][:], AF.Identity,
            bias=bneg_sb[:, i2:i2 + 1], scale=rstd_sb[:, i2:i2 + 1]), ss, 1)
        S.wait("tensor", ev_xntm)
        S.wait("tensor", ufree["pt"])
        for c in range(NCH):
            ev_T[i] = S.op("tensor", lambda E, i2=i, c2=c: E.transpose(
                ps_pt[:, c2 * P:(c2 + 1) * P], xntm_sb[i2 % 2][:, c2 * P:(c2 + 1) * P],
                ident_sb[:]), st, 1)
        S.wait("scalar", ev_T[i])
        if i >= 4:
            S.wait("scalar", ev_xnfm_wr[i - 4])
        if i >= 3:
            S.wait("scalar", ev_xnp[i - 3])
        dstv = xnfm_sb[b4][:].rearrange("p c t -> p (c t)")
        ev_xnfm[i] = S.op("scalar", lambda E, d2=dstv: E.copy(d2, ps_pt[:]), ss, 1)
        ufree["pt"] = ev_xnfm[i]
        srcv = xnfm_sb[b4][:].rearrange("p c t -> p (c t)")
        ev_xnfm_wr[i] = S.op("scalar", lambda E, ii2=i, s=srcv: E.dma_start(
            out=xnfm_scr[:, ii2 * D:(ii2 + 1) * D], in_=s), f"ws{i % 4}", 16)

    def prep_mix(g, ii):
        i = 4 * g + ii
        b4 = i % 4
        S.wait("vector", ev_xnfm[i])
        if i == 0:
            S.op("vector", lambda E: E.memset(xnprev_sb[:, :, 0:1], 0.0))
        else:
            S.op("vector", lambda E, pb=(i - 1) % 4: E.tensor_copy(
                xnprev_sb[:, :, 0:1], xnfm_sb[pb][:, :, P - 1:P]))
        ev = S.op("vector", lambda E, bb2=b4: E.tensor_copy(
            xnprev_sb[:, :, 1:P], xnfm_sb[bb2][:, :, 0:P - 1]), sv, 1)
        ev_xnp[i] = ev
        S.wait("vector", ev)
        S.op("vector", lambda E, bb2=b4: E.tensor_sub(
            dmix_sb[:], xnfm_sb[bb2][:], xnprev_sb[:]))
        if i >= 4:
            S.wait("vector", ev_rmm[i - 4])
        if g >= 2 and ii == 0:
            S.wait("vector", ev_kmm[(g - 2, NCH - 1)])
        S.op("vector", lambda E, m3=murb3: E.tensor_tensor(
            tmp_fm[:], dmix_sb[:], m3, op=OP.mult))
        ev_xr[i] = S.op("vector", lambda E, b4=i % 4: E.tensor_add(
            xr_sb[b4][:], tmp_fm[:], xnprev_sb[:]), sv, 1)
        S.op("vector", lambda E, m3=mukb3: E.tensor_tensor(
            tmp_fm[:], dmix_sb[:], m3, op=OP.mult))
        ev_xk_tile[i] = S.op("vector", lambda E, gg=g % 2, ii2=ii: E.tensor_add(
            xk_sb[gg][:, :, ii2 * P:(ii2 + 1) * P], tmp_fm[:], xnprev_sb[:]), sv, 1)

    for ii0 in range(4):
        prep_scalar(0, ii0)
        prep_mix(0, ii0)

    for g in range(NG):
        tiles = [4 * g + ii for ii in range(4)]

        for ii in range(4):
            i = tiles[ii]
            bb = i % 2
            S.wait("tensor", ev_xr[i])
            if i >= 1:
                S.wait("tensor", ev_sig[i - 1])
            if g >= 1 and ii == 0:
                S.wait("tensor", ev_ffn[4 * (g - 1) + 2])
            ev = None
            for c in range(NCH):
                for h in range(2):
                    ev = S.op("tensor", lambda E, b4=i % 4, c2=c, h2=h: E.matmul(
                        ps_sA[:, h2 * 512:(h2 + 1) * 512], xr_sb[b4][:, c2, :],
                        wrt_sb[:, c2, h2 * 512:(h2 + 1) * 512],
                        start=(c2 == 0), stop=(c2 == NCH - 1)), st, 1)
            ev_rmm[i] = ev
            S.wait("scalar", ev_rmm[i])
            if i >= 4:
                S.wait("scalar", ev_ffn[i - 4])
            ev_sig[i] = S.op("scalar", lambda E, bb4=i % 4: E.activation(
                rbf_sb[bb4][:], ps_sA[:], AF.Sigmoid), ss, 1)

        if g == 0:
            S.wait("tensor", ("lw2", 16))
            S.wait("tensor", ("lw3", 16))
        S.wait("tensor", ev_xk_tile[tiles[3]])
        if g >= 1:
            S.wait("tensor", ev_ffn[4 * (g - 1) + 1])
        for d in range(NCH):
            if d >= 2:
                S.wait("tensor", ev_kevac[(g, d - 2)])
            elif g >= 1:
                S.wait("tensor", ev_kevac[(g - 1, 6 + d)])
            ev = None
            for c in range(NCH):
                ev = S.op("tensor", lambda E, gg=g % 2, c2=c, d2=d: E.matmul(
                    ps_sB[:, (d2 % 2) * 512:(d2 % 2 + 1) * 512],
                    wkt_sb[:, c2, d2 * P:(d2 + 1) * P],
                    xk_sb[gg][:, c2, :], start=(c2 == 0), stop=(c2 == NCH - 1)), st, 1)
            ev_kmm[(g, d)] = ev
            S.wait("scalar", ev_kmm[(g, d)])
            rb = d % 2
            pk = (g, d - 2) if d >= 2 else ((g - 1, d + 6) if g >= 1 else None)
            if pk is not None and pk in ev_kevac:
                S.wait("scalar", ev_kevac[pk])
            evrl = S.op("scalar", lambda E, d2=d, r2=rb: E.activation(
                relu2_sb[:, r2, :], ps_sB[:, (d2 % 2) * 512:(d2 % 2 + 1) * 512],
                AF.Relu), ss, 1)
            S.wait("scalar", evrl)
            ev_kevac[(g, d)] = S.op("scalar", lambda E, d2=d, r2=rb: E.activation(
                kact_sb[:, d2, :], relu2_sb[:, r2, :], AF.Square), ss, 1)

        for ii in range(4):
            i = tiles[ii]
            bb = i % 2
            ps_v = vp_ps[ii]
            S.wait("tensor", ev_kevac[(g, NCH - 1)])
            if ii == 0 and g >= 1:
                S.wait("tensor", ev_ffn[4 * (g - 1) + 3])
            elif ii == 1:
                S.wait("tensor", ev_kevac[(g, NCH - 1)])
                if True:
                    pass
            elif ii == 2:
                S.wait("tensor", ev_sig[tiles[3]])
            elif ii == 3:
                S.wait("tensor", ev_ffn[tiles[0]])
            ev = None
            for d in range(NCH):
                for h in range(2):
                    ev = S.op("tensor", lambda E, pv=ps_v, ii2=ii, d2=d, h2=h: E.matmul(
                        pv[:, h2 * 512:(h2 + 1) * 512],
                        kact_sb[:, d2, ii2 * P:(ii2 + 1) * P],
                        wvt_sb[:, d2, h2 * 512:(h2 + 1) * 512],
                        start=(d2 == 0), stop=(d2 == NCH - 1)), st, 1)
            ev_vmm[i] = ev

            S.wait("scalar", ev_x_load[i])
            S.wait("scalar", ev_stats_g[g])
            ev_pp = S.op("scalar", lambda E, bb2=bb, ii2=i: E.activation(
                part_sb[bb2][:], px_tm[ii2 % 8][:], AF.Identity,
                bias=bneg_sb[:, ii2:ii2 + 1], scale=a1_sb[:, ii2:ii2 + 1]), ss, 1)
            pxbuf_free[i % 8] = [ev_pp, ev_red[i]]

            if i + 8 < ntiles:
                emit_px_load(i + 8)
            js = i + 7
            if 7 <= js < ntiles:
                emit_px_stats(js)
                if js % 4 == 3:
                    stats_tail(js // 4)

            if g + 1 < NG:
                prep_scalar(g + 1, ii)

            S.wait("vector", ev_vmm[i])
            S.wait("vector", ev_sig[i])
            S.wait("vector", ev_pp)
            if i >= 2:
                S.wait("vector", ev_part_wr[i - 2])
            evf = None
            for h in range(2):
                S.op("vector", lambda E, pv=ps_v, bb4=i % 4, h2=h: E.tensor_mul(
                    relu_sb[:], rbf_sb[bb4][:, h2 * 512:(h2 + 1) * 512],
                    pv[:, h2 * 512:(h2 + 1) * 512]))
                evf = S.op("vector", lambda E, bb2=bb, h2=h: E.tensor_add(
                    comb_sb[bb2][:, h2 * 512:(h2 + 1) * 512],
                    part_sb[bb2][:, h2 * 512:(h2 + 1) * 512], relu_sb[:]), sv, 1)
            ev_ffn[i] = evf

            S.wait("scalar", ev_ffn[i])
            ev_part_wr[i] = S.op("scalar", lambda E, ii2=i, bb2=bb: E.dma_start(
                out=part_scr[:, ii2 * D:(ii2 + 1) * D], in_=comb_sb[bb2][:]), f"wp{i % 2}", 16)

            if g + 1 < NG:
                prep_mix(g + 1, ii)
    ufree["sA"] = ev_ffn[4 * (NG - 1) + 2]
    ufree["sB"] = ev_ffn[4 * (NG - 1) + 1]
    ufree["av"] = ev_ffn[4 * (NG - 1) + 3]

    if debug:
        d_xk = nc.dram_tensor("d_xk", [P, NCH, 512], BF16, kind="ExternalOutput").ap()
        d_kact = nc.dram_tensor("d_kact", [P, NCH, 512], BF16, kind="ExternalOutput").ap()
        d_rbf = nc.dram_tensor("d_rbf", [P, 2, D], BF16, kind="ExternalOutput").ap()
        S.wait("sync", (sv, S.cnt[sv]))
        S.wait("sync", (ss, S.cnt[ss]))
        S.op("sync", lambda E: E.dma_start(out=d_xk, in_=xk_sb[(NG - 1) % 2][:]), "wd", 16)
        S.op("sync", lambda E: E.dma_start(out=d_kact, in_=kact_sb[:]), "wd", 16)
        S.op("sync", lambda E: E.dma_start(out=d_rbf[:, 0, :], in_=rbf_sb[0][:]), "wd", 16)
        S.op("sync", lambda E: E.dma_start(out=d_rbf[:, 1, :], in_=rbf_sb[1][:]), "wd", 16)

    for sname in ("ws0", "ws1", "ws2", "ws3", "wp0", "wp1", "wd"):
        if sname in S.cnt:
            S.wait("sync", (sname, S.cnt[sname]))

    xnq_free = [None, None]
    acc_stage_free = [None, None]
    ev_out_wr = [None] * ntiles
    p_free = [None, None]
    pt_free = [None, None]
    gbc = 0

    def emit_xnq_load(g):
        S.wait("sync", xnq_free[g % 2])
        dst = xnq_sb[g % 2][:].rearrange("p a c t -> p (a c t)")
        return S.op("sync", lambda E, gg=g, d2=dst: E.dma_start(
            out=d2, in_=xnfm_scr[:, gg * 4 * D:(gg + 1) * 4 * D]), f"lq{g % 2}", 16)

    def emit_k_unit(g, kf, ev_xnq_l):
        u = "sA" if kf % 2 == 0 else "sB"
        S.wait("tensor", ev_xnq_l)
        S.wait("tensor", ufree[u])
        ev = None
        for c in range(NCH):
            ev = S.op("tensor", lambda E, gg=g % 2, c2=c, kf2=kf, hp=pair_ps[kf % 2]: E.matmul(
                hp[:, 0:512], wt_sb[:, c2, kf2 * P:(kf2 + 1) * P],
                xnq_sb[gg][:, :, c2, :], start=(c2 == 0), stop=(c2 == NCH - 1)), st, 1)
        S.wait("scalar", ev)
        evk = S.op("scalar", lambda E, gg=g, kf2=kf, hp=pair_ps[kf % 2]: E.copy(
            kfm_sb[:, kf2, gg * 512:(gg + 1) * 512], hp[:, 0:512]), ss, 1)
        ufree[u] = evk

    def emit_v_unit(g, sub, ev_xnq_l):
        S.wait("tensor", ev_xnq_l)
        S.wait("tensor", ufree["av"])
        ev = None
        for c in range(NCH):
            for h in range(2):
                ev = S.op("tensor", lambda E, gg=g % 2, s2=sub, c2=c, h2=h: E.matmul(
                    ps_av[:, h2 * 512:(h2 + 1) * 512], xnq_sb[gg][:, s2, c2, :],
                    wt_sb[:, c2, D + h2 * 512:D + (h2 + 1) * 512],
                    start=(c2 == 0), stop=(c2 == NCH - 1)), st, 1)
        S.wait("scalar", ev)
        evv = S.op("scalar", lambda E, gg=g, s2=sub: E.copy(
            vtm_sb[:, gg * 4 + s2, :], ps_av[:]), ss, 1)
        ufree["av"] = evv

    for g in range(NG):
        tiles = [4 * g + ii for ii in range(4)]

        if g == 0:
            ev_xnq0 = emit_xnq_load(0)
            S.wait("tensor", ("lw4", 16))
            S.wait("tensor", ev_sig[ntiles - 1])
            S.wait("tensor", ev_kevac[(NG - 1, NCH - 1)])
            for kf in range(NCH):
                emit_k_unit(0, kf, ev_xnq0)
            for sub in range(4):
                emit_v_unit(0, sub, ev_xnq0)
        kv_pending = []
        if g + 1 < NG:
            ev_xnqn = emit_xnq_load(g + 1)
            kv_pending = ([("k", kf) for kf in range(NCH)]
                          + [("v", sub) for sub in range(4)])

        def drain_kv(n, g=g, ev_xnqn=(emit_xnq_load if False else None)):
            pass

        def drain_kv(n):
            for _ in range(n):
                if not kv_pending:
                    return
                kind, idx = kv_pending.pop(0)
                if kind == "k":
                    emit_k_unit(g + 1, idx, ev_xnqn)
                else:
                    emit_v_unit(g + 1, idx, ev_xnqn)

        ctxs = []
        flat = []
        for ii in range(4):
            i = tiles[ii]
            nkc_tot = i + 1
            nb = (nkc_tot + 7) // 8
            ct = dict(ii=ii, i=i, nb=nb,
                      blocks=[(8 * j, min(8, nkc_tot - 8 * j)) for j in range(nb)],
                      bpar=[(gbc + j) % 2 for j in range(nb)],
                      slot=[(gbc + j) % 8 for j in range(nb)], tp=i % 2,
                      ev_S=[None] * nb, ev_A=[None] * nb, ev_exp=[None] * nb,
                      ev_max=[None] * nb, ev_fold=[None] * nb, ev_ptv=[None] * nb,
                      ev_part_l=None)
            gbc += nb
            ctxs.append(ct)
            flat += [(ii, j) for j in range(nb)]

        def emit_scores(ct, j, g=g):
            ii, i, nb, tp = ct["ii"], ct["i"], ct["nb"], ct["tp"]
            kc0, nkc = ct["blocks"][j]
            w = nkc * P
            jp = ct["bpar"][j]
            diag = (j == nb - 1)
            un = "sA" if jp == 0 else "sB"
            S.wait("tensor", ufree[un])
            wmm = w - P if diag else w
            spans = []
            if wmm > 0:
                spans.append((0, min(512, wmm)))
                if wmm > 512:
                    spans.append((512, wmm - 512))
            ev = None
            ev_span = [None] * max(1, len(spans))
            for c in range(NCH):
                for si, (off, n) in enumerate(spans):
                    ev = S.op("tensor", lambda E, gg=g % 2, ii3=ii, c2=c, o2=off,
                              n2=n, jp2=jp, kc=kc0: E.matmul(
                        pair_ps[jp2][:, o2:o2 + n2], xnq_sb[gg][:, ii3, c2, :],
                        kfm_sb[:, c2, kc * P + o2:kc * P + o2 + n2],
                        start=(c2 == 0), stop=(c2 == NCH - 1)), st, 1)
                    if c == NCH - 1:
                        ev_span[si] = ev
            if diag:
                S.wait("tensor", ufree["misc"])
                for c in range(NCH):
                    ev = S.op("tensor", lambda E, gg=g % 2, ii3=ii, c2=c, kd=i: E.matmul(
                        ps_misc[:, 0:P], xnq_sb[gg][:, ii3, c2, :],
                        kfm_sb[:, c2, kd * P:(kd + 1) * P],
                        start=(c2 == 0), stop=(c2 == NCH - 1)), st, 1)
            ct["ev_S"][j] = ev
            S.wait("vector", ct["ev_S"][j])
            if diag:
                doff = (i - kc0) * P
                evm = S.op("vector", lambda E, jp2=jp, d2=doff: E.tensor_add(
                    pair_ps[jp2][:, d2:d2 + P], ps_misc[:, 0:P], mask_sb[:]), sv, 1)
                ufree["misc"] = evm
                S.wait("vector", evm)
            mt = mrun_sb[:, j % 2:j % 2 + 1]
            if j == 0:
                ev = S.op("vector", lambda E, jp2=jp, w2=w, m2=mt: E.reduce_max(
                    m2, pair_ps[jp2][:, 0:w2], axis=AX), sv, 1)
                S.wait("vector", ev)
                ct["ev_max"][j] = S.op("vector", lambda E, sl=ct["slot"][j], m2=mt: E.tensor_scalar_mul(
                    nm_sb[:, sl:sl + 1], m2, -1.0), sv, 1)
            else:
                mo = mrun_sb[:, (j - 1) % 2:(j - 1) % 2 + 1]
                ev = S.op("vector", lambda E, jp2=jp, w2=w: E.reduce_max(
                    mblk_sb[:], pair_ps[jp2][:, 0:w2], axis=AX), sv, 1)
                S.wait("vector", ev)
                ev = S.op("vector", lambda E, m2=mt, mo2=mo: E.tensor_max(
                    m2, mo2, mblk_sb[:]), sv, 1)
                S.wait("vector", ev)
                ev = S.op("vector", lambda E, sl=ct["slot"][j], m2=mt, mo2=mo: E.tensor_sub(
                    dm_sb[:, sl:sl + 1], mo2, m2), sv, 1)
                S.wait("vector", ev)
                ct["ev_max"][j] = S.op("vector", lambda E, sl=ct["slot"][j], m2=mt: E.tensor_scalar_mul(
                    nm_sb[:, sl:sl + 1], m2, -1.0), sv, 1)
            S.wait("scalar", ct["ev_max"][j])
            if j > 0:
                S.op("scalar", lambda E, sl=ct["slot"][j]: E.activation(
                    scale_sb[:, sl:sl + 1], dm_sb[:, sl:sl + 1], AF.Exp))
            S.wait("scalar", p_free[jp])
            ct["ev_exp"][j] = S.op("scalar", lambda E, jp2=jp, w2=w, sl=ct["slot"][j]: E.activation(
                p_sb[jp2][:, 0:w2], pair_ps[jp2][:, 0:w2], AF.Exp,
                bias=nm_sb[:, sl:sl + 1], scale=1.0,
                accum_out=sblk_sb[:, sl:sl + 1]), ss, 1)
            ufree[un] = ct["ev_exp"][j]

        def emit_t(ct, j):
            if j == 0:
                i = ct["i"]
                if i >= 2:
                    S.wait("sync", ev_out_wr[i - 2])
                ct["ev_part_l"] = S.op("sync", lambda E, ii2=i: E.dma_start(
                    out=out_sb[ii2 % 2][:], in_=part_scr[:, ii2 * D:(ii2 + 1) * D]),
                    f"lp{i % 2}", 16)
            kc0, nkc = ct["blocks"][j]
            jp = ct["bpar"][j]
            h0n = (nkc + 1) // 2
            S.wait("tensor", ct["ev_exp"][j])
            S.wait("tensor", ufree["pt"])
            ev = None
            for kc in range(nkc):
                ev = S.op("tensor", lambda E, jp2=jp, k2=kc: E.transpose(
                    ps_pt[:, k2 * P:(k2 + 1) * P], p_sb[jp2][:, k2 * P:(k2 + 1) * P],
                    ident_sb[:]), st, 1)
            p_free[jp] = ev
            S.wait("vector", ev)
            S.wait("vector", pt_free[jp])
            ev_h0 = S.op("vector", lambda E, jp2=jp, w2=h0n * P: E.tensor_copy(
                pt_sb[jp2][:, 0:w2], ps_pt[:, 0:w2]), sv, 1)
            if nkc > h0n:
                ev_h1 = S.op("vector", lambda E, jp2=jp, a=h0n * P, w2=nkc * P: E.tensor_copy(
                    pt_sb[jp2][:, a:w2], ps_pt[:, a:w2]), sv, 1)
            else:
                ev_h1 = ev_h0
            ct["ev_ptv"][j] = (ev_h0, ev_h1, h0n)
            ufree["pt"] = ev_h1

        def emit_a(ct, j):
            ii, i, nb, tp = ct["ii"], ct["i"], ct["nb"], ct["tp"]
            kc0, nkc = ct["blocks"][j]
            jp = ct["bpar"][j]
            ev_h0, ev_h1, h0n = ct["ev_ptv"][j]
            S.wait("tensor", ev_h0)
            S.wait("tensor", ufree["av"])
            ev = None
            for kc in range(nkc):
                if kc == h0n:
                    S.wait("tensor", ev_h1)
                for h in range(2):
                    ev = S.op("tensor", lambda E, jp2=jp, k2=kc, h2=h, kg=kc0 + kc: E.matmul(
                        ps_av[:, h2 * 512:(h2 + 1) * 512], pt_sb[jp2][:, k2 * P:(k2 + 1) * P],
                        vtm_sb[:, kg, h2 * 512:(h2 + 1) * 512],
                        start=(k2 == 0), stop=(k2 == nkc - 1)), st, 1)
            ct["ev_A"][j] = ev
            pt_free[jp] = ev
            if nb > 1:
                S.wait("vector", ct["ev_A"][j])
                if j == 0:
                    S.wait("vector", acc_stage_free[0])
                    acc_stage_free[0] = None
                    ev = S.op("vector", lambda E: E.tensor_copy(acc_sb[0][:], ps_av[:]), sv, 1)
                    S.wait("vector", ev)
                    ct["ev_fold"][j] = S.op("vector", lambda E, sl=ct["slot"][0]: E.tensor_copy(
                        srun_sb[0][:], sblk_sb[:, sl:sl + 1]), sv, 1)
                    S.wait("vector", ct["ev_fold"][j])
                else:
                    S.wait("vector", acc_stage_free[j % 2])
                    acc_stage_free[j % 2] = None
                    ev = S.op("vector", lambda E, j2=j, sl=ct["slot"][j]: E.scalar_tensor_tensor(
                        acc_sb[j2 % 2][:], acc_sb[(j2 - 1) % 2][:],
                        scale_sb[:, sl:sl + 1], ps_av[:],
                        op0=OP.mult, op1=OP.add), sv, 1)
                    S.wait("vector", ev)
                    ct["ev_fold"][j] = S.op("vector", lambda E, j2=j, sl=ct["slot"][j]: E.scalar_tensor_tensor(
                        srun_sb[j2 % 2][:], srun_sb[(j2 - 1) % 2][:],
                        scale_sb[:, sl:sl + 1],
                        sblk_sb[:, sl:sl + 1], op0=OP.mult, op1=OP.add), sv, 1)
                    S.wait("vector", ct["ev_fold"][j])
                ufree["av"] = ct["ev_fold"][j]

        def emit_final(ct):
            ii, i, nb, tp = ct["ii"], ct["i"], ct["nb"], ct["tp"]
            S.wait("vector", ct["ev_part_l"])
            stage = (i % 2) if nb == 1 else (nb % 2)
            S.wait("vector", acc_stage_free[stage])
            acc_stage_free[stage] = None
            if nb == 1:
                S.wait("vector", ct["ev_A"][0])
                ev = S.op("vector", lambda E, sl=ct["slot"][0]: E.reciprocal(rs_sb[:], sblk_sb[:, sl:sl + 1]), sv, 1)
                S.wait("vector", ev)
                ev_out = S.op("vector", lambda E, sg=stage, bb2=i % 2: E.scalar_tensor_tensor(
                    acc_sb[sg][:], ps_av[:], rs_sb[:], out_sb[bb2][:],
                    op0=OP.mult, op1=OP.add), sv, 1)
                ufree["av"] = ev_out
            else:
                ev = S.op("vector", lambda E, pl=(nb - 1) % 2: E.reciprocal(rs_sb[:], srun_sb[pl][:]), sv, 1)
                S.wait("vector", ev)
                ev_out = S.op("vector", lambda E, sg=stage, bb2=i % 2, pl=(nb - 1) % 2: E.scalar_tensor_tensor(
                    acc_sb[sg][:], acc_sb[pl][:], rs_sb[:], out_sb[bb2][:],
                    op0=OP.mult, op1=OP.add), sv, 1)
            S.wait("scalar", ev_out)
            ev_out_wr[i] = S.op("scalar", lambda E, ii2=i, sg=stage: E.dma_start(
                out=y_d[ii2 * P:(ii2 + 1) * P, :], in_=acc_sb[sg][:]), f"wo{i % 2}", 16)
            acc_stage_free[stage] = ev_out_wr[i]

        emit_scores(ctxs[flat[0][0]], flat[0][1])
        if len(flat) > 1:
            emit_scores(ctxs[flat[1][0]], flat[1][1])
        per_step = max(1, -(-len(kv_pending) // max(1, len(flat) - 2)))
        for b, (ii, j) in enumerate(flat):
            ct = ctxs[ii]
            emit_t(ct, j)
            if b + 2 < len(flat):
                ii2, j2 = flat[b + 2]
                emit_scores(ctxs[ii2], j2)
            emit_a(ct, j)
            if b >= 2:
                drain_kv(per_step)
            if j == ct["nb"] - 1:
                emit_final(ct)
        drain_kv(len(kv_pending))

        last_ct = ctxs[3]
        xnq_free[g % 2] = last_ct["ev_S"][last_ct["nb"] - 1]

    if debug:
        for sname in ("wo0", "wo1"):
            if sname in S.cnt:
                S.wait("sync", (sname, S.cnt[sname]))
        S.wait("sync", (sv, S.cnt[sv]))
        d_mean = nc.dram_tensor("d_mean", [P, ntiles], F32, kind="ExternalOutput").ap()
        d_rstd = nc.dram_tensor("d_rstd", [P, ntiles], F32, kind="ExternalOutput").ap()
        d_s2 = nc.dram_tensor("d_s2", [P, ntiles], F32, kind="ExternalOutput").ap()
        d_veps = nc.dram_tensor("d_veps", [P, ntiles], F32, kind="ExternalOutput").ap()
        S.op("sync", lambda E: E.dma_start(out=d_mean, in_=mean_sb[:]), "wd", 16)
        S.op("sync", lambda E: E.dma_start(out=d_rstd, in_=rstd_sb[:]), "wd", 16)
        S.op("sync", lambda E: E.dma_start(out=d_s2, in_=s2_sb[:]), "wd", 16)
        S.op("sync", lambda E: E.dma_start(out=d_veps, in_=veps_sb[:]), "wd", 16)
        d_m2 = nc.dram_tensor("d_m2", [P, ntiles], F32, kind="ExternalOutput").ap()
        d_std = nc.dram_tensor("d_std", [P, ntiles], F32, kind="ExternalOutput").ap()
        S.op("sync", lambda E: E.dma_start(out=d_m2, in_=m2_sb[:]), "wd", 16)
        S.op("sync", lambda E: E.dma_start(out=d_std, in_=std_sb[:]), "wd", 16)
        d_xnfm = nc.dram_tensor("d_xnfm", [P, ntiles * D], BF16, kind="ExternalOutput").ap()
        d_part = nc.dram_tensor("d_part", [P, ntiles * D], F32, kind="ExternalOutput").ap()
        d_kfm = nc.dram_tensor("d_kfm", [P, NCH, T], BF16, kind="ExternalOutput").ap()
        d_vtm = nc.dram_tensor("d_vtm", [P, ntiles, D], BF16, kind="ExternalOutput").ap()
        for sname in ("wo0", "wo1"):
            if sname in S.cnt:
                S.wait("sync", (sname, S.cnt[sname]))
        S.wait("sync", (sv, S.cnt[sv]))
        S.op("sync", lambda E: E.dma_start(out=d_xnfm, in_=xnfm_scr[:, :]), "wd", 16)
        S.op("sync", lambda E: E.dma_start(out=d_part, in_=part_scr[:, :]), "wd", 16)
        S.op("sync", lambda E: E.dma_start(out=d_kfm, in_=kfm_sb[:]), "wd", 16)
        S.op("sync", lambda E: E.dma_start(out=d_vtm, in_=vtm_sb[:]), "wd", 16)

    for sname in ("wo0", "wo1", "wd"):
        if sname in S.cnt:
            S.wait("sync", (sname, S.cnt[sname]))

    from contextlib import ExitStack
    with ExitStack() as ctx:
        e = ctx.enter_context
        names = ["lc", "lw", "lw2", "lw3", "lw4", "st", "sv", "ss", "wd",
                 "px0", "px1", "px2", "px3",
                 "lx0", "lx1", "lx2", "lx3", "lf0", "lf1", "lq0", "lq1",
                 "lp0", "lp1", "ws0", "ws1", "ws2", "ws3", "wp0", "wp1", "wo0", "wo1"]
        sems = {n: e(nc.semaphore(n)) for n in names if n in S.cnt or n in
                {w[1] for q in S.q.values() for w in q if w[0] == "w"}}
        block = e(nc.Block())
        block.sync(lambda E: S.play("sync", E, sems))
        block.gpsimd(lambda E: S.play("gpsimd", E, sems))
        block.tensor(lambda E: S.play("tensor", E, sems))
        block.vector(lambda E: S.play("vector", E, sems))
        block.scalar(lambda E: S.play("scalar", E, sems))
    return nc


_NC_CACHE = {}


def kernel(L0_attn_x, L0_attn_W, L0_ffn_W_r, L0_ffn_W_k, L0_ffn_W_v,
           L0_ffn_mu_r, L0_ffn_mu_k):
    x = np.ascontiguousarray(np.asarray(L0_attn_x, dtype=np.float32))
    B, T, Dm = x.shape
    ntiles = T // P
    w = np.asarray(L0_attn_W, dtype=np.float32)
    wrT = np.ascontiguousarray(np.asarray(L0_ffn_W_r, dtype=np.float32).T)
    wkT = np.ascontiguousarray(np.asarray(L0_ffn_W_k, dtype=np.float32).T)
    wvT = np.ascontiguousarray(np.asarray(L0_ffn_W_v, dtype=np.float32).T)
    wT = np.ascontiguousarray(w.T)
    mur8 = np.ascontiguousarray(np.asarray(L0_ffn_mu_r, dtype=np.float32).reshape(NCH, P).T)
    muk8 = np.ascontiguousarray(np.asarray(L0_ffn_mu_k, dtype=np.float32).reshape(NCH, P).T)
    ident = np.eye(P, dtype=ml_dtypes.bfloat16)
    mask = np.triu(np.full((P, P), NEG, dtype=np.float32), k=1)

    if ntiles not in _NC_CACHE:
        _NC_CACHE[ntiles] = build_program(ntiles)
    nc = _NC_CACHE[ntiles]

    in_maps = []
    for b in range(B):
        in_maps.append(dict(
            x=x[b], wT=wT, wrT=wrT, wkT=wkT,
            wvT=wvT, mur8=mur8, muk8=muk8, ident=ident, mask=mask))
    res = run_bass_kernel_spmd(nc, in_maps, core_ids=list(range(B)))
    return np.stack([np.asarray(r["y"]) for r in res.results], axis=0)


if __name__ == "__main__":
    rng = np.random.default_rng(0)
    B, T = 2, 512
    xs = rng.standard_normal((B, T, D), dtype=np.float32)
    W = rng.standard_normal((2 * D, D), dtype=np.float32) * 0.02
    s = 1.0 / np.sqrt(D)
    Wr = rng.uniform(-s, s, (D, D)).astype(np.float32)
    Wk = rng.uniform(-s, s, (D, D)).astype(np.float32)
    Wv = rng.uniform(-s, s, (D, D)).astype(np.float32)
    mur = rng.uniform(0, 1, D).astype(np.float32)
    muk = rng.uniform(0, 1, D).astype(np.float32)
    out = kernel(xs, W, Wr, Wk, Wv, mur, muk)
    print(out.shape, out.dtype)
